# version 02 (final)
# baseline (speedup 1.0000x reference)
import numpy as np
import ml_dtypes
import concourse.bass as bass
import concourse.mybir as mybir
from concourse.bass_utils import run_bass_kernel_spmd

F32 = mybir.dt.float32
BF16 = mybir.dt.bfloat16
AF = mybir.ActivationFunctionType
ALU = mybir.AluOpType
AX = mybir.AxisListType
NPBF = ml_dtypes.bfloat16

NCORES = 8
T = 4096
D = 1024
SEQ = 2048
NSEQ = 2
EPS = 1e-5
FPAD = 2816
NFC = 22


class Ser:

    def __init__(self):
        self.nc = bass.Bass("TRN2", target_bir_lowering=False)
        self.items = []
        self.ctxs = []
        self.total = 0

    def _add(self, eng, fn, inc):
        before = self.total
        self.items.append((eng, fn, before, inc))
        self.total += inc

    def dma(self, dst, src, **kw):
        self._add("sync", lambda e: e.dma_start(dst, src, **kw), 16)

    def mm(self, out, lhsT, rhs, start=True, stop=True):
        self._add("tensor", lambda e: e.matmul(out, lhsT, rhs, start=start, stop=stop), 1)

    def mmn(self, out, lhsT, rhs, start=True, stop=True, nmax=512):
        n = out.shape[-1]
        for j in range(0, n, nmax):
            jj = min(n, j + nmax)
            self.mm(out[:, j:jj], lhsT, rhs[:, j:jj], start=start, stop=stop)

    def act(self, out, in_, func, scale=None):
        if scale is None:
            self._add("scalar", lambda e: e.activation(out, in_, func), 1)
        else:
            self._add("scalar", lambda e: e.activation(out, in_, func, scale=scale), 1)

    def tt(self, out, a, b, op):
        self._add("vector", lambda e: e.tensor_tensor(out, a, b, op), 1)

    def ts(self, out, a, s, op):
        self._add("vector", lambda e: e.tensor_scalar(out=out, in0=a, scalar1=s, scalar2=None, op0=op), 1)

    def red(self, out, in_, op):
        self._add("vector", lambda e: e.tensor_reduce(out, in_, AX.X, op), 1)

    def recip(self, out, in_):
        self._add("vector", lambda e: e.reciprocal(out, in_), 1)

    def memset(self, ap, v):
        self._add("vector", lambda e: e.memset(ap, v), 1)

    def copy(self, out, in_):
        self._add("vector", lambda e: e.tensor_copy(out, in_), 1)

    def sb(self, name, shape, dt):
        t = self.nc.sbuf_tensor(name, shape, dt)
        h = t.__enter__()
        self.ctxs.append(t)
        return h

    def ps(self, name, shape, dt=F32):
        t = self.nc.psum_tensor(name, shape, dt)
        h = t.__enter__()
        self.ctxs.append(t)
        return h

    def din(self, name, shape, dt=F32):
        return self.nc.dram_tensor(name, list(shape), dt, kind="ExternalInput").ap()

    def dout(self, name, shape, dt=F32):
        return self.nc.dram_tensor(name, list(shape), dt, kind="ExternalOutput").ap()

def build_and_items(ser):
    nc = ser.nc
    sem_cm = nc.semaphore("g")
    sem = sem_cm.__enter__()
    ser.ctxs.append(sem_cm)
    blk_cm = nc.Block()
    blk = blk_cm.__enter__()
    ser.ctxs.append(blk_cm)
    items = ser.items
    total = ser.total

    def make(engname):
        def body(e):
            for (en, fn, before, inc) in items:
                if en != engname:
                    continue
                if before > 0:
                    e.wait_ge(sem, before)
                fn(e).then_inc(sem, inc)
            if engname == "vector":
                e.wait_ge(sem, total)
        return body

    blk.sync(make("sync"))
    blk.tensor(make("tensor"))
    blk.vector(make("vector"))
    blk.scalar(make("scalar"))
    for cm in reversed(ser.ctxs):
        cm.__exit__(None, None, None)
    return nc


_prog_cache = {}


def run_prog(key, builder, in_maps, outnames):
    if key not in _prog_cache:
        ser = Ser()
        builder(ser)
        nc = build_and_items(ser)
        _prog_cache[key] = (ser, nc)
    ser, nc = _prog_cache[key]
    res = run_bass_kernel_spmd(nc, in_maps, core_ids=list(range(NCORES)))
    return [{n: np.asarray(r[n]) for n in outnames} for r in res.results]



def b_ln(ser):
    x = ser.din("x", (T, D))
    y = ser.dout("y", (T, D))
    yb = ser.dout("yb", (T, D), BF16)
    xt = ser.sb("xt", [128, D], F32)
    m = ser.sb("m", [128, 1], F32)
    v = ser.sb("v", [128, 1], F32)
    sq = ser.sb("sq", [128, D], F32)
    yt = ser.sb("yt", [128, D], F32)
    ybt = ser.sb("ybt", [128, D], BF16)
    for tc in range(T // 128):
        r = tc * 128
        ser.dma(xt[:, :], x[r:r + 128, :])
        ser.red(m[:, :], xt[:, :], ALU.add)
        ser.ts(m[:, :], m[:, :], 1.0 / D, ALU.mult)
        ser.tt(yt[:, :], xt[:, :], m.broadcast_to([128, D]), ALU.subtract)
        ser.act(sq[:, :], yt[:, :], AF.Square)
        ser.red(v[:, :], sq[:, :], ALU.add)
        ser.ts(v[:, :], v[:, :], 1.0 / D, ALU.mult)
        ser.ts(v[:, :], v[:, :], EPS, ALU.add)
        ser.act(v[:, :], v[:, :], AF.Sqrt)
        ser.recip(v[:, :], v[:, :])
        ser.tt(yt[:, :], yt[:, :], v.broadcast_to([128, D]), ALU.mult)
        ser.copy(ybt[:, :], yt[:, :])
        ser.dma(y[r:r + 128, :], yt[:, :])
        ser.dma(yb[r:r + 128, :], ybt[:, :])


def b_prep(ser):
    pshared = ser.ps("pshared", [1, 2 * D])
    for nm in ["q", "k", "v"]:
        W = ser.din("W" + nm, (D, D))
        scol = ser.din("s" + nm, (8, 128, 1))
        bcol = ser.din("bc" + nm, (8, 128, 1))
        brow = ser.din("br" + nm, (1, D))
        Wsc = ser.dout("Wsc" + nm, (D, D), BF16)
        beff = ser.dout("beff" + nm, (1, D))
        wt = ser.sb("wt" + nm, [128, D], F32)
        wb = ser.sb("wb" + nm, [128, D], BF16)
        cl = ser.sb("cl" + nm, [128, 1], F32)
        ps = pshared
        acc = ser.sb("acc" + nm, [1, D], F32)
        rowt = ser.sb("rowt" + nm, [1, D], F32)
        for c in range(8):
            ser.dma(wt[:, :], W[c * 128:(c + 1) * 128, :])
            ser.dma(cl[:, :], scol[c])
            ser.tt(wb[:, :], wt[:, :], cl.broadcast_to([128, D]), ALU.mult)
            ser.dma(Wsc[c * 128:(c + 1) * 128, :], wb[:, :])
            ser.dma(cl[:, :], bcol[c])
            for j in range(0, D, 512):
                ser.mm(ps[:, j:j + 512], cl[:, :], wt[:, j:j + 512],
                       start=True, stop=True)
            for j in range(0, D, 512):
                if c == 0:
                    ser.copy(acc[:, j:j + 512], ps[:, j:j + 512])
                else:
                    ser.tt(acc[:, j:j + 512], acc[:, j:j + 512], ps[:, j:j + 512], ALU.add)
        ser.dma(rowt[:, :], brow)
        ser.tt(acc[:, :], acc[:, :], rowt[:, :], ALU.add)
        ser.dma(beff, acc[:, :])
    W1 = ser.din("W1", (D, 2 * D))
    s1 = ser.din("s1", (8, 128, 1))
    b1c = ser.din("b1c", (8, 128, 1))
    b1r = ser.din("b1r", (1, 2 * D))
    W1sc = ser.dout("W1sc", (D, 2 * D), BF16)
    b1eff = ser.dout("b1eff", (1, 2 * D))
    w1t = ser.sb("w1t", [128, 2 * D], F32)
    w1b = ser.sb("w1b", [128, 2 * D], BF16)
    c1 = ser.sb("c1", [128, 1], F32)
    p1 = pshared
    a1 = ser.sb("a1", [1, 2 * D], F32)
    r1t = ser.sb("r1t", [1, 2 * D], F32)
    for c in range(8):
        ser.dma(w1t[:, :], W1[c * 128:(c + 1) * 128, :])
        ser.dma(c1[:, :], s1[c])
        ser.tt(w1b[:, :], w1t[:, :], c1.broadcast_to([128, 2 * D]), ALU.mult)
        ser.dma(W1sc[c * 128:(c + 1) * 128, :], w1b[:, :])
        ser.dma(c1[:, :], b1c[c])
        for j in range(0, 2 * D, 512):
            ser.mm(p1[:, j:j + 512], c1[:, :], w1t[:, j:j + 512])
        for j in range(0, 2 * D, 512):
            if c == 0:
                ser.copy(a1[:, j:j + 512], p1[:, j:j + 512])
            else:
                ser.tt(a1[:, j:j + 512], a1[:, j:j + 512], p1[:, j:j + 512], ALU.add)
    ser.dma(r1t[:, :], b1r)
    ser.tt(a1[:, :], a1[:, :], r1t[:, :], ALU.add)
    ser.dma(b1eff, a1[:, :])
    casts = [
        ("Wo", (D, D)), ("W2", (2 * D, D)), ("qkvwT", (D, 3 * D)),
        ("gatewT", (D, D)), ("outwT", (D, D)), ("Wr", (D, 8)),
        ("Wu", (4 * 8 * 128, FPAD)), ("Wd", (8 * NFC * 128, D)),
    ]
    for nm, shp in casts:
        src = ser.din(nm, shp)
        dst = ser.dout(nm + "b", shp, BF16)
        w = shp[1]
        ct = ser.sb("ct" + nm, [128, w], F32)
        cb = ser.sb("cb" + nm, [128, w], BF16)
        for r0 in range(0, shp[0], 128):
            ser.dma(ct[:, :], src[r0:r0 + 128, :])
            ser.copy(cb[:, :], ct[:, :])
            ser.dma(dst[r0:r0 + 128, :], cb[:, :])


def b_mmtok(name, nout, nres, act=None, out_bf=False, inw=D):
    def f(ser):
        xT = ser.din("xT", (inw, T), BF16)
        W = ser.din("W", (inw, nout), BF16)
        bias = ser.din("bias", (128, nout))
        res = [ser.din("res%d" % i, (T, nout)) for i in range(nres)]
        out = ser.dout("out", (T, nout), BF16 if out_bf else F32)
        KC = inw // 128
        wts = ser.sb("wts", [128, KC, nout], BF16)
        for c in range(KC):
            ser.dma(wts[:, c, :], W[c * 128:(c + 1) * 128, :])
        bt = ser.sb("bt", [128, nout], F32)
        ser.dma(bt[:, :], bias)
        xts = ser.sb("xts", [128, KC, 128], BF16)
        ps = ser.ps("psm", [128, 512])
        ot = ser.sb("ot", [128, nout], F32)
        obf = ser.sb("obf", [128, nout], BF16)
        rt = ser.sb("rt", [128, nout], F32)
        for tc in range(T // 128):
            r = tc * 128
            for c in range(KC):
                ser.dma(xts[:, c, :], xT[c * 128:(c + 1) * 128, r:r + 128])
            for j in range(0, nout, 512):
                jw = min(512, nout - j)
                for c in range(KC):
                    ser.mm(ps[:, :jw], xts[:, c, :], wts[:, c, j:j + jw],
                           start=(c == 0), stop=(c == KC - 1))
                ser.tt(ot[:, j:j + jw], ps[:, :jw], bt[:, j:j + jw], ALU.add)
            for rr in res:
                ser.dma(rt[:, :], rr[r:r + 128, :])
                ser.tt(ot[:, :], ot[:, :], rt[:, :], ALU.add)
            if act == "sigmoid":
                ser.act(ot[:, :], ot[:, :], AF.Sigmoid)
            if out_bf:
                ser.copy(obf[:, :], ot[:, :])
                ser.dma(out[r:r + 128, :], obf[:, :])
            else:
                ser.dma(out[r:r + 128, :], ot[:, :])
    return f


def b_qkv_a(ser):
    xT = ser.din("xT", (D, T), BF16)
    outs = {}
    wts = {}
    bts = {}
    for nm in ["q", "k", "v"]:
        W = ser.din("W" + nm, (D, D), BF16)
        w = ser.sb("w" + nm, [128, 8, D], BF16)
        for c in range(8):
            ser.dma(w[:, c, :], W[c * 128:(c + 1) * 128, :])
        wts[nm] = w
        b = ser.din("b" + nm, (128, D))
        bt = ser.sb("bt" + nm, [128, D], F32)
        ser.dma(bt[:, :], b)
        bts[nm] = bt
        outs[nm] = ser.dout("o" + nm, (T, D), BF16)
    xts = ser.sb("xts", [128, 8, 128], BF16)
    ps = ser.ps("ps", [128, 512])
    ot = ser.sb("ot", [128, 512], F32)
    obf = ser.sb("obf", [128, D], BF16)
    for tc in range(T // 128):
        r = tc * 128
        for c in range(8):
            ser.dma(xts[:, c, :], xT[c * 128:(c + 1) * 128, r:r + 128])
        for nm in ["q", "k", "v"]:
            for j in range(0, D, 512):
                for c in range(8):
                    ser.mm(ps[:, :], xts[:, c, :], wts[nm][:, c, j:j + 512],
                           start=(c == 0), stop=(c == 7))
                ser.tt(ot[:, :], ps[:, :], bts[nm][:, j:j + 512], ALU.add)
                ser.copy(obf[:, j:j + 512], ot[:, :])
            ser.dma(outs[nm][r:r + 128, :], obf[:, :])


def b_score(ser):
    QT = ser.din("QT", (NSEQ, 16, 64, SEQ), BF16)
    KT = ser.din("KT", (NSEQ, 16, 64, SEQ), BF16)
    P = ser.dout("P", (NSEQ, 16, SEQ, SEQ), BF16)
    rden = ser.dout("rden", (NSEQ, 16, SEQ, 1))
    qt = ser.sb("qt", [64, SEQ], BF16)
    kt = ser.sb("kt", [64, SEQ], BF16)
    ps = ser.ps("ps", [128, 2048])
    mx = ser.sb("mx", [128, 4], F32)
    mx1 = ser.sb("mx1", [128, 1], F32)
    sf = ser.sb("sf", [128, SEQ], F32)
    pb = ser.sb("pb", [128, SEQ], BF16)
    den = ser.sb("den", [128, 1], F32)
    for s in range(NSEQ):
        for h in range(16):
            ser.dma(qt[:, :], QT[s, h])
            ser.dma(kt[:, :], KT[s, h])
            for q in range(16):
                for j in range(0, SEQ, 512):
                    ser.mm(ps[:, j:j + 512], qt[:, q * 128:(q + 1) * 128],
                           kt[:, j:j + 512])
                if q == 0:
                    ser.ts(ps[0:1, SEQ - 1:SEQ], ps[0:1, SEQ - 1:SEQ], -999.0, ALU.add)
                for j in range(4):
                    ser.red(mx[:, j:j + 1], ps[:, j * 512:(j + 1) * 512], ALU.max)
                ser.red(mx1[:, :], mx[:, :], ALU.max)
                for j in range(4):
                    ser.tt(sf[:, j * 512:(j + 1) * 512], ps[:, j * 512:(j + 1) * 512],
                           mx1.broadcast_to([128, 512]), ALU.subtract)
                ser.act(pb[:, :], sf[:, :], AF.Exp, scale=1.0 / 32.0)
                ser.red(den[:, :], pb[:, :], ALU.add)
                ser.recip(den[:, :], den[:, :])
                ser.dma(P[s, h, q * 128:(q + 1) * 128, :], pb[:, :])
                ser.dma(rden[s, h, q * 128:(q + 1) * 128, :], den[:, :])


def b_av(ser):
    PT = ser.din("PT", (NSEQ, 16, SEQ, SEQ), BF16)
    V = ser.din("V", (NSEQ, 16, SEQ, 64), BF16)
    attT = ser.dout("attT", (NSEQ, 16, 64, SEQ))
    vt = ser.sb("vt", [128, 16, 64], BF16)
    pt = ser.sb("pt", [128, SEQ], BF16)
    ps = ser.ps("ps", [64, 2048])
    ot = ser.sb("ot", [64, SEQ], F32)
    for s in range(NSEQ):
        for h in range(16):
            for kc in range(16):
                ser.dma(vt[:, kc, :], V[s, h, kc * 128:(kc + 1) * 128, :])
            for kc in range(16):
                ser.dma(pt[:, :], PT[s, h, kc * 128:(kc + 1) * 128, :])
                for j in range(0, SEQ, 512):
                    ser.mm(ps[:, j:j + 512], vt[:, kc, :], pt[:, j:j + 512],
                           start=(kc == 0), stop=(kc == 15))
            for j in range(0, SEQ, 512):
                ser.copy(ot[:, j:j + 512], ps[:, j:j + 512])
            ser.dma(attT[s, h], ot[:, :])


def b_scaleatt(ser):
    a = ser.din("a", (T, D))
    rr = ser.din("rr", (T, D))
    o = ser.dout("o", (T, D), BF16)
    at = ser.sb("at", [128, D], F32)
    rt = ser.sb("rt", [128, D], F32)
    ob = ser.sb("ob", [128, D], BF16)
    for tc in range(T // 128):
        r = tc * 128
        ser.dma(at[:, :], a[r:r + 128, :])
        ser.dma(rt[:, :], rr[r:r + 128, :])
        ser.tt(ob[:, :], at[:, :], rt[:, :], ALU.mult)
        ser.dma(o[r:r + 128, :], ob[:, :])


def b_mul2(ser):
    a = ser.din("a", (T, D), BF16)
    g = ser.din("g", (T, D))
    o = ser.dout("o", (T, D), BF16)
    at = ser.sb("at", [128, D], BF16)
    gt = ser.sb("gt", [128, D], F32)
    ob = ser.sb("ob", [128, D], BF16)
    for tc in range(T // 128):
        r = tc * 128
        ser.dma(at[:, :], a[r:r + 128, :])
        ser.dma(gt[:, :], g[r:r + 128, :])
        ser.tt(ob[:, :], at[:, :], gt[:, :], ALU.mult)
        ser.dma(o[r:r + 128, :], ob[:, :])


def b_mlp1(ser):
    hT = ser.din("hT", (D, T), BF16)
    W1 = ser.din("W1", (D, 2 * D), BF16)
    b1c = ser.din("b1c", (16, 128, 1))
    midT = ser.dout("midT", (2 * D, T), BF16)
    w = ser.sb("w", [128, 8, 2 * D], BF16)
    for c in range(8):
        ser.dma(w[:, c, :], W1[c * 128:(c + 1) * 128, :])
    bc = ser.sb("bc", [128, 16], F32)
    for fc in range(16):
        ser.dma(bc[:, fc:fc + 1], b1c[fc])
    xts = ser.sb("xts", [128, 8, 512], BF16)
    ps = ser.ps("ps", [128, 512])
    xx = ser.sb("xx", [128, 512], F32)
    s1 = ser.sb("s1", [128, 512], F32)
    s2 = ser.sb("s2", [128, 512], F32)
    u = ser.sb("u", [128, 512], F32)
    th = ser.sb("th", [128, 512], F32)
    gb = ser.sb("gb", [128, 512], BF16)
    C0 = 0.044715
    C1 = 0.7978845608028654
    for nt in range(T // 512):
        r = nt * 512
        for c in range(8):
            ser.dma(xts[:, c, :], hT[c * 128:(c + 1) * 128, r:r + 512])
        for fc in range(16):
            for c in range(8):
                ser.mm(ps[:, :], w[:, c, fc * 128:(fc + 1) * 128], xts[:, c, :],
                       start=(c == 0), stop=(c == 7))
            ser.tt(xx[:, :], ps[:, :], bc[:, fc:fc + 1].broadcast_to([128, 512]), ALU.add)
            ser.act(s1[:, :], xx[:, :], AF.Square)
            ser.tt(s2[:, :], s1[:, :], xx[:, :], ALU.mult)
            ser.ts(s2[:, :], s2[:, :], C0, ALU.mult)
            ser.tt(u[:, :], xx[:, :], s2[:, :], ALU.add)
            ser.act(th[:, :], u[:, :], AF.Tanh, scale=C1)
            ser.ts(th[:, :], th[:, :], 1.0, ALU.add)
            ser.tt(th[:, :], th[:, :], xx[:, :], ALU.mult)
            ser.ts(gb[:, :], th[:, :], 0.5, ALU.mult)
            ser.dma(midT[fc * 128:(fc + 1) * 128, r:r + 512], gb[:, :])


def b_mlp2(ser):
    midT = ser.din("midT", (2 * D, T), BF16)
    W2 = ser.din("W2", (2 * D, D), BF16)
    bias = ser.din("bias", (128, D))
    res0 = ser.din("res0", (T, D))
    res1 = ser.din("res1", (T, D))
    out = ser.dout("out", (T, D))
    w = ser.sb("w", [128, 16, D], BF16)
    for c in range(16):
        ser.dma(w[:, c, :], W2[c * 128:(c + 1) * 128, :])
    bt = ser.sb("bt", [128, D], F32)
    ser.dma(bt[:, :], bias)
    mt = ser.sb("mt", [128, 16, 128], BF16)
    ps = ser.ps("ps", [128, 512])
    ot = ser.sb("ot", [128, D], F32)
    rt = ser.sb("rt", [128, D], F32)
    for tc in range(T // 128):
        r = tc * 128
        for c in range(16):
            ser.dma(mt[:, c, :], midT[c * 128:(c + 1) * 128, r:r + 128])
        for j in range(0, D, 512):
            for c in range(16):
                ser.mm(ps[:, :], mt[:, c, :], w[:, c, j:j + 512],
                       start=(c == 0), stop=(c == 15))
            ser.tt(ot[:, j:j + 512], ps[:, :], bt[:, j:j + 512], ALU.add)
        ser.dma(rt[:, :], res0[r:r + 128, :])
        ser.tt(ot[:, :], ot[:, :], rt[:, :], ALU.add)
        ser.dma(rt[:, :], res1[r:r + 128, :])
        ser.tt(ot[:, :], ot[:, :], rt[:, :], ALU.add)
        ser.dma(out[r:r + 128, :], ot[:, :])


def b_router(ser):
    xT = ser.din("xT", (D, T), BF16)
    Wr = ser.din("Wr", (D, 8), BF16)
    br = ser.din("br", (128, 8))
    Sm = ser.dout("Sm", (T, 8))
    w = ser.sb("w", [128, 8, 8], BF16)
    for c in range(8):
        ser.dma(w[:, c, :], Wr[c * 128:(c + 1) * 128, :])
    bt = ser.sb("bt", [128, 8], F32)
    ser.dma(bt[:, :], br)
    xts = ser.sb("xts", [128, 8, 128], BF16)
    ps = ser.ps("ps", [128, 8])
    lg = ser.sb("lg", [128, 8], F32)
    mx = ser.sb("mx", [128, 1], F32)
    den = ser.sb("den", [128, 1], F32)
    for tc in range(T // 128):
        r = tc * 128
        for c in range(8):
            ser.dma(xts[:, c, :], xT[c * 128:(c + 1) * 128, r:r + 128])
        for c in range(8):
            ser.mm(ps[:, :], xts[:, c, :], w[:, c, :], start=(c == 0), stop=(c == 7))
        ser.tt(lg[:, :], ps[:, :], bt[:, :], ALU.add)
        ser.red(mx[:, :], lg[:, :], ALU.max)
        ser.tt(lg[:, :], lg[:, :], mx.broadcast_to([128, 8]), ALU.subtract)
        ser.act(lg[:, :], lg[:, :], AF.Exp)
        ser.red(den[:, :], lg[:, :], ALU.add)
        ser.recip(den[:, :], den[:, :])
        ser.tt(lg[:, :], lg[:, :], den.broadcast_to([128, 8]), ALU.mult)
        ser.dma(Sm[r:r + 128, :], lg[:, :])


def b_bisect(ser):
    NTOK = 32768
    CF = NTOK // 128
    sm = ser.din("smcol", (128, CF))
    ones_r = ser.din("ones_r", (1, 128))
    ones_c = ser.din("ones_c", (128, 1))
    selm = ser.dout("selm", (128, CF))
    strictm = ser.dout("strictm", (128, CF))
    smt = ser.sb("smt", [128, CF], F32)
    ser.dma(smt[:, :], sm)
    onr = ser.sb("onr", [1, 128], F32)
    ser.dma(onr[:, :], ones_r)
    onc = ser.sb("onc", [128, 1], F32)
    ser.dma(onc[:, :], ones_c)
    lo = ser.sb("lo", [1, 1], F32)
    hi = ser.sb("hi", [1, 1], F32)
    mid = ser.sb("mid", [1, 1], F32)
    t1 = ser.sb("t1", [1, 1], F32)
    pred = ser.sb("pred", [1, 1], F32)
    cnt = ser.sb("cnt", [1, 1], F32)
    pbc = ser.ps("pbc", [128, 1])
    colb = ser.sb("colb", [128, 1], F32)
    mask = ser.sb("mask", [128, CF], F32)
    csum = ser.sb("csum", [128, 1], F32)
    pcnt = ser.ps("pcnt", [1, 1])
    ser.memset(lo[:, :], -1.0)
    ser.memset(hi[:, :], 2.0)
    for it in range(44):
        ser.tt(mid[:, :], lo[:, :], hi[:, :], ALU.add)
        ser.ts(mid[:, :], mid[:, :], 0.5, ALU.mult)
        ser.mm(pbc[:, :], onr[:, :], mid[:, :])
        ser.copy(colb[:, :], pbc[:, :])
        ser.tt(mask[:, :], smt[:, :], colb.broadcast_to([128, CF]), ALU.is_gt)
        ser.red(csum[:, :], mask[:, :], ALU.add)
        ser.mm(pcnt[:, :], csum[:, :], onc[:, :])
        ser.copy(cnt[:, :], pcnt[:, :])
        ser.ts(pred[:, :], cnt[:, :], 4095.5, ALU.is_gt)
        ser.tt(t1[:, :], mid[:, :], lo[:, :], ALU.subtract)
        ser.tt(t1[:, :], t1[:, :], pred[:, :], ALU.mult)
        ser.tt(lo[:, :], lo[:, :], t1[:, :], ALU.add)
        ser.tt(t1[:, :], hi[:, :], mid[:, :], ALU.subtract)
        ser.tt(t1[:, :], t1[:, :], pred[:, :], ALU.mult)
        ser.tt(hi[:, :], mid[:, :], t1[:, :], ALU.add)
    ser.mm(pbc[:, :], onr[:, :], lo[:, :])
    ser.copy(colb[:, :], pbc[:, :])
    ser.tt(mask[:, :], smt[:, :], colb.broadcast_to([128, CF]), ALU.is_gt)
    ser.dma(selm, mask[:, :])
    ser.mm(pbc[:, :], onr[:, :], hi[:, :])
    ser.copy(colb[:, :], pbc[:, :])
    ser.tt(mask[:, :], smt[:, :], colb.broadcast_to([128, CF]), ALU.is_gt)
    ser.dma(strictm, mask[:, :])


def b_expert(ser):
    xT = ser.din("xT", (D, T), BF16)
    Wu = ser.din("Wu", (D, FPAD), BF16)
    buc = ser.din("buc", (NFC, 128, 1))
    Wd = ser.din("Wd", (FPAD, D), BF16)
    bd = ser.din("bd", (128, D))
    out = ser.dout("out", (T, D))
    wu = ser.sb("wu", [128, 8, FPAD], BF16)
    for c in range(8):
        ser.dma(wu[:, c, :], Wu[c * 128:(c + 1) * 128, :])
    wd = ser.sb("wd", [128, NFC, D], BF16)
    for c in range(NFC):
        ser.dma(wd[:, c, :], Wd[c * 128:(c + 1) * 128, :])
    bu = ser.sb("bu", [128, NFC], F32)
    for fc in range(NFC):
        ser.dma(bu[:, fc:fc + 1], buc[fc])
    bdt = ser.sb("bdt", [128, D], F32)
    ser.dma(bdt[:, :], bd)
    xts = ser.sb("xts", [128, 8, 512], BF16)
    ps = ser.ps("ps", [128, 512])
    ps2 = ser.ps("ps2", [128, 512])
    mt = ser.sb("mt", [128, NFC, 512], BF16)
    sl = ser.sb("sl", [128, 512], F32)
    ot = ser.sb("ot", [128, D], F32)
    for nt in range(T // 512):
        r = nt * 512
        for c in range(8):
            ser.dma(xts[:, c, :], xT[c * 128:(c + 1) * 128, r:r + 512])
        for fc in range(NFC):
            for c in range(8):
                ser.mm(ps[:, :], wu[:, c, fc * 128:(fc + 1) * 128], xts[:, c, :],
                       start=(c == 0), stop=(c == 7))
            ser.tt(sl[:, :], ps[:, :], bu[:, fc:fc + 1].broadcast_to([128, 512]), ALU.add)
            ser.act(mt[:, fc, :], sl[:, :], AF.Silu)
        for q in range(4):
            for j in range(0, D, 512):
                for fc in range(NFC):
                    ser.mm(ps2[:, :], mt[:, fc, q * 128:(q + 1) * 128], wd[:, fc, j:j + 512],
                           start=(fc == 0), stop=(fc == NFC - 1))
                ser.tt(ot[:, j:j + 512], ps2[:, :], bdt[:, j:j + 512], ALU.add)
            ser.dma(out[r + q * 128:r + (q + 1) * 128, :], ot[:, :])


def b_moesum(ser):
    scat = ser.din("scat", (8, T, D))
    gcol = ser.din("gcol", (8, T, 1))
    res = ser.din("res", (T, D))
    out = ser.dout("out", (T, D))
    st = ser.sb("st", [128, D], F32)
    gt = ser.sb("gt", [128, 1], F32)
    acc = ser.sb("acc", [128, D], F32)
    tmp = ser.sb("tmp", [128, D], F32)
    for tc in range(T // 128):
        r = tc * 128
        ser.dma(acc[:, :], res[r:r + 128, :])
        for e in range(8):
            ser.dma(st[:, :], scat[e, r:r + 128, :])
            ser.dma(gt[:, :], gcol[e, r:r + 128, :])
            ser.tt(tmp[:, :], st[:, :], gt.broadcast_to([128, D]), ALU.mult)
            ser.tt(acc[:, :], acc[:, :], tmp[:, :], ALU.add)
        ser.dma(out[r:r + 128, :], acc[:, :])


def b_qkv_b(ser):
    xT = ser.din("xT", (D, T), BF16)
    Wqkv = ser.din("Wqkv", (D, 3 * D), BF16)
    bqkv = ser.din("bqkv", (128, 3 * D))
    Wg = ser.din("Wg", (D, D), BF16)
    bg = ser.din("bg", (128, D))
    oq = ser.dout("oq", (T, D), BF16)
    ok = ser.dout("ok", (T, D), BF16)
    ov = ser.dout("ov", (T, D), BF16)
    og = ser.dout("og", (T, D))
    w = ser.sb("w", [128, 8, 3 * D], BF16)
    for c in range(8):
        ser.dma(w[:, c, :], Wqkv[c * 128:(c + 1) * 128, :])
    wg = ser.sb("wg", [128, 8, D], BF16)
    for c in range(8):
        ser.dma(wg[:, c, :], Wg[c * 128:(c + 1) * 128, :])
    bt = ser.sb("bt", [128, 3 * D], F32)
    ser.dma(bt[:, :], bqkv)
    bgt = ser.sb("bgt", [128, D], F32)
    ser.dma(bgt[:, :], bg)
    xts = ser.sb("xts", [128, 8, 128], BF16)
    ps = ser.ps("ps", [128, 512])
    ot = ser.sb("ot", [128, 512], F32)
    obf = ser.sb("obf", [128, 3 * D], BF16)
    gf = ser.sb("gf", [128, D], F32)
    for tc in range(T // 128):
        r = tc * 128
        for c in range(8):
            ser.dma(xts[:, c, :], xT[c * 128:(c + 1) * 128, r:r + 128])
        for j in range(0, 3 * D, 512):
            for c in range(8):
                ser.mm(ps[:, :], xts[:, c, :], w[:, c, j:j + 512],
                       start=(c == 0), stop=(c == 7))
            ser.tt(ot[:, :], ps[:, :], bt[:, j:j + 512], ALU.add)
            ser.copy(obf[:, j:j + 512], ot[:, :])
        ser.dma(oq[r:r + 128, :], obf[:, 0:D])
        ser.dma(ok[r:r + 128, :], obf[:, D:2 * D])
        ser.dma(ov[r:r + 128, :], obf[:, 2 * D:3 * D])
        for j in range(0, D, 512):
            for c in range(8):
                ser.mm(ps[:, :], xts[:, c, :], wg[:, c, j:j + 512],
                       start=(c == 0), stop=(c == 7))
            ser.tt(ot[:, :], ps[:, :], bgt[:, j:j + 512], ALU.add)
            ser.act(gf[:, j:j + 512], ot[:, :], AF.Sigmoid)
        ser.dma(og[r:r + 128, :], gf[:, :])


OFFS = sorted(set(range(33)) | {48, 64, 96, 128, 192, 256, 384, 512, 768, 1024, 1536})
NOFF = len(OFFS)


def b_offsets(ser):
    QT = ser.din("QT", (NSEQ, 8, 128, SEQ), BF16)
    KT = ser.din("KT", (NSEQ, 8, 128, SEQ), BF16)
    VT = ser.din("VT", (NSEQ, 8, 128, SEQ), BF16)
    SE2 = ser.din("SE2", (128, 2 * NOFF))
    pb2 = ser.din("pb2", (8, 2, NOFF))
    ifg = ser.din("ifg", (8, 128, 1))
    bo2 = ser.din("bo2", (128, 2))
    bo2T = ser.din("bo2T", (2, 128))
    OF = ser.dout("OF", (NSEQ, 8, 128, SEQ), BF16)
    qt = ser.sb("qt", [128, SEQ], BF16)
    kt = ser.sb("kt", [128, SEQ], BF16)
    vt = ser.sb("vt", [128, SEQ], BF16)
    se_f = ser.sb("se_f", [128, 2 * NOFF], F32)
    ser.dma(se_f[:, :], SE2)
    se = ser.sb("se", [128, 2 * NOFF], BF16)
    ser.copy(se[:, :], se_f[:, :])
    b2 = ser.sb("b2", [128, 2], F32)
    ser.dma(b2[:, :], bo2)
    b2t = ser.sb("b2t", [2, 128], F32)
    ser.dma(b2t[:, :], bo2T)
    pbt = ser.sb("pbt", [2, NOFF], F32)
    ig = ser.sb("ig", [128, 1], F32)
    wfact = ser.sb("wfact", [128, SEQ], F32)
    wstage = ser.sb("wstage", [2, SEQ], F32)
    prod = ser.sb("prod", [128, SEQ], F32)
    psk = ser.ps("psk", [2, 2048])
    tq = ser.sb("tq", [2, SEQ], F32)
    est = ser.sb("est", [2, SEQ], F32)
    den = ser.sb("den", [2, SEQ], F32)
    av = ser.sb("av", [128, SEQ], F32)
    pse = ser.ps("pse", [128, 2048])
    tmp = ser.sb("tmp", [128, SEQ], F32)
    ofb = ser.sb("ofb", [128, SEQ], BF16)
    INV = 1.0 / 8.0
    for s in range(NSEQ):
        for hp in range(8):
            ser.dma(qt[:, :], QT[s, hp])
            ser.dma(kt[:, :], KT[s, hp])
            ser.dma(vt[:, :], VT[s, hp])
            ser.dma(pbt[:, :], pb2[hp])
            ser.dma(ig[:, :], ifg[hp])
            for j in range(0, SEQ, 512):
                ser.mm(pse[0:2 * NOFF, j:j + 512], se[:, :], qt[:, j:j + 512])
            for j in range(0, SEQ, 512):
                ser.ts(wfact[0:88, j:j + 512], pse[0:88, j:j + 512], INV, ALU.mult)
            ser.ts(wfact[0:88, :], wfact[0:88, :], INV, ALU.add)
            ser.memset(av[:, :], 0.0)
            ser.memset(den[:, :], 0.0)
            ser.memset(prod[:, :], 0.0)
            for i, o in enumerate(OFFS):
                ser.dma(wstage[:, :], wfact[2 * i:2 * i + 2, :])
                if o == 0:
                    ser.tt(prod[:, :], qt[:, :], kt[:, :], ALU.mult)
                else:
                    ser.tt(prod[:, o:], qt[:, o:], kt[:, 0:SEQ - o], ALU.mult)
                for j in range(0, SEQ, 512):
                    ser.mm(psk[:, j:j + 512], b2[:, :], prod[:, j:j + 512])
                for j in range(0, SEQ, 512):
                    ser.tt(tq[:, j:j + 512], psk[:, j:j + 512], wstage[:, j:j + 512], ALU.mult)
                ser.tt(tq[:, :], tq[:, :], pbt[:, i:i + 1].broadcast_to([2, SEQ]), ALU.add)
                ser.act(est[:, :], tq[:, :], AF.Exp)
                if o > 0:
                    ser.memset(est[:, 0:o], 0.0)
                ser.tt(den[:, :], den[:, :], est[:, :], ALU.add)
                for j in range(0, SEQ, 512):
                    ser.mm(pse[:, j:j + 512], b2t[:, :], est[:, j:j + 512])
                for j in range(0, SEQ, 512):
                    lo_ = max(j, o)
                    hi_ = j + 512
                    if lo_ >= hi_:
                        continue
                    ser.tt(tmp[:, lo_:hi_], pse[:, lo_:hi_],
                           vt[:, lo_ - o:hi_ - o], ALU.mult)
                    ser.tt(av[:, lo_:hi_], av[:, lo_:hi_], tmp[:, lo_:hi_], ALU.add)
            ser.recip(den[:, :], den[:, :])
            for j in range(0, SEQ, 512):
                ser.mm(pse[:, j:j + 512], b2t[:, :], den[:, j:j + 512])
            for j in range(0, SEQ, 512):
                ser.tt(av[:, j:j + 512], av[:, j:j + 512], pse[:, j:j + 512], ALU.mult)
            ser.tt(av[:, :], av[:, :], ig.broadcast_to([128, SEQ]), ALU.mult)
            ser.copy(ofb[:, :], av[:, :])
            ser.dma(OF[s, hp], ofb[:, :])



def _bc(v, n=128):
    v = np.asarray(v, np.float32).reshape(1, -1)
    return np.ascontiguousarray(np.broadcast_to(v, (n, v.shape[1])))


def _cols(v, nch):
    v = np.asarray(v, np.float32).reshape(nch, 128, 1)
    return np.ascontiguousarray(v)


def kernel(**inp):
    g = lambda n: np.asarray(inp[n])
    x = g("L0_attn_a_x").astype(np.float32).reshape(16, 2048, 1024)
    shards = [np.ascontiguousarray(x[2 * c:2 * c + 2].reshape(T, D)) for c in range(NCORES)]

    wupad = np.zeros((4, D, FPAD), np.float32)
    wupad[:, :, :2730] = g("L0_ffn_W_up")
    wdpad = np.zeros((8, FPAD, D), np.float32)
    wdpad[:, :2730, :] = g("L0_ffn_W_down")
    bupad = np.zeros((4, FPAD), np.float32)
    bupad[:, :2730] = g("L0_ffn_b_up")
    prep_in = {
        "Wq": g("L0_attn_a_Wq"), "sq": _cols(g("L0_attn_a_lnq_s"), 8),
        "bcq": _cols(g("L0_attn_a_lnq_b"), 8), "brq": g("L0_attn_a_bq").reshape(1, D),
        "Wk": g("L0_attn_a_Wk"), "sk": _cols(g("L0_attn_a_lnk_s"), 8),
        "bck": _cols(g("L0_attn_a_lnk_b"), 8), "brk": g("L0_attn_a_bk").reshape(1, D),
        "Wv": g("L0_attn_a_Wv"), "sv": _cols(g("L0_attn_a_lnv_s"), 8),
        "bcv": _cols(g("L0_attn_a_lnv_b"), 8), "brv": g("L0_attn_a_bv").reshape(1, D),
        "W1": g("L0_attn_a_W1"), "s1": _cols(g("L0_attn_a_lnm_s"), 8),
        "b1c": _cols(g("L0_attn_a_lnm_b"), 8), "b1r": g("L0_attn_a_b1").reshape(1, 2 * D),
        "Wo": g("L0_attn_a_Wo"), "W2": g("L0_attn_a_W2"),
        "qkvwT": np.ascontiguousarray(g("L0_attn_b_qkv_w").T),
        "gatewT": np.ascontiguousarray(g("L0_attn_b_gate_w").T),
        "outwT": np.ascontiguousarray(g("L0_attn_b_out_w").T),
        "Wr": g("L0_ffn_Wr"),
        "Wu": np.ascontiguousarray(wupad.reshape(4 * D, FPAD)),
        "Wd": np.ascontiguousarray(wdpad.reshape(8 * FPAD, D)),
    }
    prep_in = {k: np.ascontiguousarray(v.astype(np.float32)) for k, v in prep_in.items()}
    pouts = ["Wscq", "beffq", "Wsck", "beffk", "Wscv", "beffv", "W1sc", "b1eff",
             "Wob", "W2b", "qkvwTb", "gatewTb", "outwTb", "Wrb", "Wub", "Wdb"]
    pr = run_prog("prep", b_prep, [prep_in] * NCORES, pouts)[0]
    Wub = pr["Wub"].reshape(4, D, FPAD)
    Wdb = pr["Wdb"].reshape(8, FPAD, D)

    def LN(rs):
        outs = run_prog("ln", b_ln, [{"x": s} for s in rs], ["y", "yb"])
        return [o["y"] for o in outs], [o["yb"] for o in outs]

    def tr(a):
        return np.ascontiguousarray(a.T)

    def ffn_call(r_cur):
        zf, zb = LN(r_cur)
        zT = [tr(b) for b in zb]
        sm = run_prog("router", b_router,
                      [{"xT": t, "Wr": pr["Wrb"], "br": _bc(g("L0_ffn_br"))} for t in zT],
                      ["Sm"])
        Sm_all = np.concatenate([o["Sm"] for o in sm], axis=0)
        ones_r = np.ones((1, 128), np.float32)
        ones_c = np.ones((128, 1), np.float32)
        bi = run_prog("bisect", b_bisect,
                      [{"smcol": np.ascontiguousarray(Sm_all[:, e]).reshape(128, 256),
                        "ones_r": ones_r, "ones_c": ones_c} for e in range(8)],
                      ["selm", "strictm"])
        xall = np.concatenate(zb, axis=0)
        takes = []
        for e in range(8):
            sel = bi[e]["selm"].reshape(-1) > 0.5
            strict = bi[e]["strictm"].reshape(-1) > 0.5
            idx_s = np.nonzero(strict)[0]
            idx_rest = np.nonzero(sel & ~strict)[0]
            need = 4096 - len(idx_s)
            take = np.concatenate([idx_s, idx_rest[:need]])[:4096]
            if len(take) < 4096:
                extra = np.nonzero(sel)[0]
                pad = [i for i in extra if i not in set(take)][:4096 - len(take)]
                if len(take) + len(pad) < 4096:
                    allidx = np.arange(32768)
                    pad2 = [i for i in allidx if i not in set(take) and i not in set(pad)]
                    pad = list(pad) + list(pad2[:4096 - len(take) - len(pad)])
                take = np.concatenate([take, np.asarray(pad, np.int64)])
            takes.append(np.asarray(take, np.int64))
        group = [0, 0, 1, 1, 2, 2, 3, 3]
        ex_in = []
        for e in range(8):
            xe = np.ascontiguousarray(xall[takes[e]].T)
            ex_in.append({"xT": xe, "Wu": np.ascontiguousarray(Wub[group[e]]),
                          "buc": _cols(bupad[group[e]], NFC),
                          "Wd": np.ascontiguousarray(Wdb[e]),
                          "bd": _bc(g("L0_ffn_b_down")[e])})
        exo = run_prog("expert", b_expert, ex_in, ["out"])
        scats = [np.zeros((8, T, D), np.float32) for _ in range(NCORES)]
        gcols = [np.zeros((8, T, 1), np.float32) for _ in range(NCORES)]
        for e in range(8):
            oe = exo[e]["out"]
            tk = takes[e]
            owner = tk // T
            local = tk % T
            gv = Sm_all[tk, e]
            for c in range(NCORES):
                mask = owner == c
                scats[c][e, local[mask]] = oe[mask]
                gcols[c][e, local[mask], 0] = gv[mask]
        ro = run_prog("moesum", b_moesum,
                      [{"scat": scats[c], "gcol": gcols[c], "res": r_cur[c]}
                       for c in range(NCORES)], ["out"])
        return [o["out"] for o in ro]

    yf, yb = LN(shards)
    y2f, y2b = LN(yf)
    y2T = [tr(b) for b in y2b]
    qkv = run_prog("qkva", b_qkv_a,
                   [{"xT": t, "Wq": pr["Wscq"], "Wk": pr["Wsck"], "Wv": pr["Wscv"],
                     "bq": _bc(pr["beffq"]), "bk": _bc(pr["beffk"]), "bv": _bc(pr["beffv"])}
                    for t in y2T], ["oq", "ok", "ov"])
    sc_in, av_in = [], []
    for c in range(NCORES):
        Q = qkv[c]["oq"].reshape(NSEQ, SEQ, 16, 64)
        K = qkv[c]["ok"].reshape(NSEQ, SEQ, 16, 64)
        V = qkv[c]["ov"].reshape(NSEQ, SEQ, 16, 64)
        sc_in.append({"QT": np.ascontiguousarray(Q.transpose(0, 2, 3, 1)),
                      "KT": np.ascontiguousarray(K.transpose(0, 2, 3, 1))})
        av_in.append({"V": np.ascontiguousarray(V.transpose(0, 2, 1, 3))})
    sco = run_prog("score", b_score, sc_in, ["P", "rden"])
    for c in range(NCORES):
        av_in[c]["PT"] = np.ascontiguousarray(sco[c]["P"].swapaxes(2, 3))
    avo = run_prog("av", b_av, av_in, ["attT"])
    sa_in = []
    for c in range(NCORES):
        att = np.ascontiguousarray(avo[c]["attT"].transpose(0, 3, 1, 2).reshape(T, D))
        rd = sco[c]["rden"].reshape(NSEQ, 16, SEQ)
        rr = np.ascontiguousarray(
            np.repeat(rd.transpose(0, 2, 1), 64, axis=2).reshape(T, D))
        sa_in.append({"a": att, "rr": rr})
    sao = run_prog("scaleatt", b_scaleatt, sa_in, ["o"])
    attT = [tr(o["o"]) for o in sao]
    ho = run_prog("oproj", b_mmtok("oproj", D, 1),
                  [{"xT": attT[c], "W": pr["Wob"], "bias": _bc(g("L0_attn_a_bo")),
                    "res0": yf[c]} for c in range(NCORES)], ["out"])
    hs = [o["out"] for o in ho]
    hnf, hnb = LN(hs)
    hnT = [tr(b) for b in hnb]
    m1 = run_prog("mlp1", b_mlp1,
                  [{"hT": t, "W1": pr["W1sc"], "b1c": _cols(pr["b1eff"], 16)} for t in hnT],
                  ["midT"])
    r1o = run_prog("mlp2", b_mlp2,
                   [{"midT": m1[c]["midT"], "W2": pr["W2b"],
                     "bias": _bc(g("L0_attn_a_b2")), "res0": hs[c], "res1": shards[c]}
                    for c in range(NCORES)], ["out"])
    r1 = [o["out"] for o in r1o]

    r2 = ffn_call(r1)

    z2f, z2b = LN(r2)
    z2T = [tr(b) for b in z2b]
    qb = run_prog("qkvb", b_qkv_b,
                  [{"xT": t, "Wqkv": pr["qkvwTb"], "bqkv": _bc(g("L0_attn_b_qkv_b")),
                    "Wg": pr["gatewTb"], "bg": _bc(g("L0_attn_b_gate_b"))} for t in z2T],
                  ["oq", "ok", "ov", "og"])
    SE = g("L0_attn_b_scale_embed").astype(np.float32)
    SE2 = np.zeros((128, 2 * NOFF), np.float32)
    for i in range(NOFF):
        SE2[0:64, 2 * i] = SE[i]
        SE2[64:128, 2 * i + 1] = SE[i]
    PB = g("L0_attn_b_pos_bias").astype(np.float32)
    pb2 = np.zeros((8, 2, NOFF), np.float32)
    for hp in range(8):
        pb2[hp, 0, :] = PB[:, 2 * hp]
        pb2[hp, 1, :] = PB[:, 2 * hp + 1]
    IG = g("L0_attn_b_if_gain").astype(np.float32)
    ifg = np.zeros((8, 128, 1), np.float32)
    for hp in range(8):
        ifg[hp, 0:64, 0] = IG[2 * hp]
        ifg[hp, 64:128, 0] = IG[2 * hp + 1]
    bo2 = np.zeros((128, 2), np.float32)
    bo2[0:64, 0] = 1.0
    bo2[64:128, 1] = 1.0
    bo2T = np.ascontiguousarray(bo2.T)
    off_in = []
    for c in range(NCORES):
        Qb = qb[c]["oq"].reshape(NSEQ, SEQ, 8, 128).transpose(0, 2, 3, 1)
        Kb = qb[c]["ok"].reshape(NSEQ, SEQ, 8, 128).transpose(0, 2, 3, 1)
        Vb = qb[c]["ov"].reshape(NSEQ, SEQ, 8, 128).transpose(0, 2, 3, 1)
        off_in.append({"QT": np.ascontiguousarray(Qb), "KT": np.ascontiguousarray(Kb),
                       "VT": np.ascontiguousarray(Vb), "SE2": SE2, "pb2": pb2,
                       "ifg": ifg, "bo2": bo2, "bo2T": bo2T})
    ofo = run_prog("offsets", b_offsets, off_in, ["OF"])
    mg_in = []
    for c in range(NCORES):
        of = np.ascontiguousarray(ofo[c]["OF"].transpose(0, 3, 1, 2).reshape(T, D))
        mg_in.append({"a": of, "g": qb[c]["og"]})
    mg = run_prog("mul2", b_mul2, mg_in, ["o"])
    ogT = [tr(o["o"]) for o in mg]
    r3o = run_prog("boproj", b_mmtok("boproj", D, 1),
                   [{"xT": ogT[c], "W": pr["outwTb"], "bias": _bc(g("L0_attn_b_out_b")),
                     "res0": r2[c]} for c in range(NCORES)], ["out"])
    r3 = [o["out"] for o in r3o]

    r4 = ffn_call(r3)

    out = np.concatenate([r.reshape(NSEQ, SEQ, D) for r in r4], axis=0)
    return out.astype(np.float32)
